# version 03 (final)
# baseline (speedup 1.0000x reference)
import numpy as np
import ml_dtypes
from contextlib import ExitStack
import concourse.bass as bass
import concourse.mybir as mybir
from concourse.bass_utils import run_bass_kernel_spmd

F32 = mybir.dt.float32
BF16 = mybir.dt.bfloat16
AF = mybir.ActivationFunctionType
ALU = mybir.AluOpType
AX = mybir.AxisListType
BFD = ml_dtypes.bfloat16

QUEUES = ("tensor", "vector", "scalar", "gpsimd", "sync")

D = 2048
DT = 16
QN = 1024
KL = 1536
S = 2048
H0 = 16
FH = 5504
MH = 8192
SCALE0 = 1.0 / float(np.sqrt(128.0))
NEG = -1e30


NSYNC = 8


class Prog:
    def __init__(self, nc, stack):
        self.nc = nc
        self.sem = {q: stack.enter_context(nc.semaphore("sem_" + q))
                    for q in QUEUES if q != "sync"}
        for i in range(NSYNC):
            self.sem["sync%d" % i] = stack.enter_context(nc.semaphore("sem_sy%d" % i))
        self.cnt = {k: 0 for k in self.sem}
        self.ndma = 0
        self.pend = {q: [] for q in QUEUES}
        self.last_w = {}
        self.readers = {}
        self.wm = {q: {k: 0 for k in self.sem} for q in QUEUES}
        self.keepalive = []

    _uid = [0]

    def sbuf(self, stack, name, shape, dtype):
        self._uid[0] += 1
        t = stack.enter_context(
            self.nc.sbuf_tensor(f"{name}_u{self._uid[0]}", list(shape), dtype))
        self.keepalive.append(t)
        return t

    def psum(self, stack, name, shape, dtype):
        self._uid[0] += 1
        t = stack.enter_context(
            self.nc.psum_tensor(f"{name}_u{self._uid[0]}", list(shape), dtype))
        self.keepalive.append(t)
        return t

    def _key(self, ap):
        return id(getattr(ap, "tensor", ap))

    def emit(self, q, fn, reads=(), writes=(), dma=False):
        deps = []
        for b in reads:
            w = self.last_w.get(self._key(b))
            if w:
                deps.append(w)
        for b in writes:
            k = self._key(b)
            w = self.last_w.get(k)
            if w:
                deps.append(w)
            deps.extend(self.readers.get(k, ()))
        waits = []
        for (sk, tgt) in deps:
            if sk == q and q == "tensor":
                continue
            if tgt > self.wm[q][sk]:
                self.wm[q][sk] = tgt
                waits.append((sk, tgt))
        if dma and q == "sync":
            sk = "sync%d" % (self.ndma % NSYNC)
            self.ndma += 1
            inc = 16
        else:
            sk = q
            inc = 16 if dma else 1
        self.cnt[sk] += inc
        ev = (sk, self.cnt[sk])
        for b in writes:
            k = self._key(b)
            self.last_w[k] = ev
            self.readers[k] = []
        for b in reads:
            self.readers.setdefault(self._key(b), []).append(ev)
        self.pend[q].append((waits, fn, sk, inc))
        return ev

    def barrier(self):
        snap = dict(self.cnt)
        for q in QUEUES:
            waits = []
            for sk in self.sem:
                if sk != q and snap[sk] > self.wm[q][sk]:
                    self.wm[q][sk] = snap[sk]
                    waits.append((sk, snap[sk]))
            if waits:
                self.pend[q].append((waits, None, None, 0))

    def build(self):
        nc = self.nc
        with nc.Block() as block:
            def mk(qname):
                def body(eng):
                    for (waits, fn, sk, inc) in self.pend[qname]:
                        for (wsk, tgt) in waits:
                            eng.wait_ge(self.sem[wsk], tgt)
                        if fn is not None:
                            fn(eng).then_inc(self.sem[sk], inc)
                return body
            block.tensor(mk("tensor"))
            block.vector(mk("vector"))
            block.scalar(mk("scalar"))
            block.gpsimd(mk("gpsimd"))
            block.sync(mk("sync"))

    def ld(self, dst, src):
        self.emit("sync", lambda e: e.dma_start(out=dst, in_=src),
                  reads=[src], writes=[dst], dma=True)

    st = ld

    def mm(self, out, lhsT, rhs, start, stop):
        self.emit("tensor", lambda e: e.matmul(out, lhsT, rhs, start=start, stop=stop),
                  reads=[lhsT, rhs], writes=[out])

    def trans(self, out, in_, ident):
        self.emit("tensor", lambda e: e.transpose(out, in_, ident),
                  reads=[in_, ident], writes=[out])

    def act(self, out, in_, func, bias=None, scale=None, accum_out=None):
        kw = {}
        if bias is not None:
            kw["bias"] = bias
        if scale is not None:
            kw["scale"] = scale
        if accum_out is not None:
            kw["accum_out"] = accum_out
        reads = [in_] + ([bias] if hasattr(bias, "tensor") else []) \
                + ([scale] if hasattr(scale, "tensor") else [])
        writes = [out] + ([accum_out] if accum_out is not None else [])
        self.emit("scalar", lambda e: e.activation(out, in_, func, **kw),
                  reads=reads, writes=writes)

    def scopy(self, out, in_, scale=None):
        if scale is None:
            self.emit("scalar", lambda e: e.copy(out, in_), reads=[in_], writes=[out])
        else:
            self.act(out, in_, AF.Copy, scale=scale)

    def sadd(self, out, in_, c):
        self.emit("scalar", lambda e: e.add(out, in_, c), reads=[in_], writes=[out])

    def tt(self, out, a, b, op):
        self.emit("vector", lambda e: e.tensor_tensor(out, a, b, op),
                  reads=[a, b], writes=[out])

    def ts(self, out, in0, s1, op0, s2=None, op1=None):
        kw = {} if op1 is None else {"op1": op1}
        reads = [in0] + ([s1] if hasattr(s1, "tensor") else []) \
                + ([s2] if hasattr(s2, "tensor") else [])
        self.emit("vector", lambda e: e.tensor_scalar(out, in0, s1, s2, op0, **kw),
                  reads=reads, writes=[out])

    def stt(self, out, in0, s, in1, op0, op1, accum_out=None):
        reads = [in0, in1] + ([s] if hasattr(s, "tensor") else [])
        writes = [out] + ([accum_out] if accum_out is not None else [])
        self.emit("vector",
                  lambda e: e.scalar_tensor_tensor(out, in0, s, in1, op0, op1,
                                                   accum_out=accum_out),
                  reads=reads, writes=writes)

    def vcopy(self, out, in_):
        self.emit("vector", lambda e: e.tensor_copy(out, in_),
                  reads=[in_], writes=[out])

    def pcopy(self, out, in_):
        self.emit("gpsimd", lambda e: e.tensor_copy(out, in_),
                  reads=[in_], writes=[out])

    def recip(self, out, in_):
        self.emit("vector", lambda e: e.reciprocal(out, in_),
                  reads=[in_], writes=[out])

    def memset(self, ap, const):
        self.emit("vector", lambda e: e.memset(ap, const), reads=[], writes=[ap])

    def collective(self, kind, op, groups, in_ap, out_ap):
        self.emit("gpsimd",
                  lambda e: e.collective_compute(kind, op, replica_groups=groups,
                                                 ins=[in_ap], outs=[out_ap]),
                  reads=[in_ap], writes=[out_ap], dma=True)


def build_nc(part="fused", dbg=False):
    L0 = part in ("fused", "l0")
    L1 = part in ("fused", "l1")
    nc = bass.Bass("TRN2", target_bir_lowering=False)
    ein, eout = "ExternalInput", "ExternalOutput"
    dt = nc.dram_tensor
    xT = L0 and dt("xT", [D, KL], F32, kind=ein)
    msw = L0 and dt("msw", [8 * 128, 640], BF16, kind=ein)
    mom = L0 and dt("mom", [8 * 128, 256], BF16, kind=ein)
    m1 = L1 and dt("m1", [8 * 128, S], BF16, kind=ein)
    cos0 = L0 and dt("cos0", [128, KL], F32, kind=ein)
    sin0 = L0 and dt("sin0", [128, KL], F32, kind=ein)
    cos1k = L1 and dt("cos1k", [128, S], F32, kind=ein)
    sin1k = L1 and dt("sin1k", [128, S], F32, kind=ein)
    cos1q = L1 and dt("cos1q", [128, QN], F32, kind=ein)
    sin1q = L1 and dt("sin1q", [128, QN], F32, kind=ein)
    dect = L0 and dt("dect", [128, 256], BF16, kind=ein)
    ident = dt("identI", [128, 128], BF16, kind=ein)
    ones1f = dt("ones1f", [1, 128], F32, kind=ein)
    onescf = dt("onescf", [128, 1], F32, kind=ein)
    onescb = dt("onescb", [128, 1], BF16, kind=ein)
    n1r = L0 and dt("n1r", [128, DT], F32, kind=ein)
    n2r = L0 and dt("n2r", [128, DT], F32, kind=ein)
    sc1r = L1 and dt("sc1r", [128, DT], F32, kind=ein)
    b1r = L1 and dt("b1r", [128, DT], F32, kind=ein)
    qsc = L0 and dt("qsc", [128, 1], F32, kind=ein)
    ksc = L0 and dt("ksc", [128, 1], F32, kind=ein)
    w_qkv = L0 and dt("w_qkv", [D, 3 * D], F32, kind=ein)
    gg_w1 = L0 and dt("gg_w1", [D, 64], F32, kind=ein)
    gg_w2 = L0 and dt("gg_w2", [64, 1], F32, kind=ein)
    w_o = L0 and dt("w_o", [D, D], F32, kind=ein)
    mem_w1 = L0 and dt("mem_w1", [D, MH], F32, kind=ein)
    mem_w2 = L0 and dt("mem_w2", [MH, D], F32, kind=ein)
    ffn_w1 = L0 and dt("ffn_w1", [D, FH], F32, kind=ein)
    ffn_w3 = L0 and dt("ffn_w3", [D, FH], F32, kind=ein)
    ffn_w2 = L0 and dt("ffn_w2", [FH, D], F32, kind=ein)
    wg_t = L0 and dt("wg_t", [D, MH], F32, kind=ein)
    wu_t = L0 and dt("wu_t", [D, MH], F32, kind=ein)
    wd_t = L0 and dt("wd_t", [MH, D], F32, kind=ein)
    wq_p2 = L1 and dt("wq_p2", [D, D], F32, kind=ein)
    wk_p = L1 and dt("wk_p", [D, 512], F32, kind=ein)
    wv = L1 and dt("wv", [D, 512], F32, kind=ein)
    wo_p = L1 and dt("wo_p", [D, D], F32, kind=ein)
    w1f_t = L1 and dt("w1f_t", [D, D], F32, kind=ein)
    d_acc = L0 and dt("d_acc", [D, QN], F32)
    accout = (part == "l0") and dt("accout", [D, QN], F32, kind=eout)
    x1out = (part == "l0") and dt("x1out", [D, QN], BF16, kind=eout)
    accin = (part == "l1") and dt("accin", [D, QN], F32, kind=ein)
    x1in = (part == "l1") and dt("x1in", [2 * D, QN], BF16, kind=ein)
    ik = eout if dbg else "Internal"
    d_xnq = dt("d_xnq", [D, QN], BF16, kind=ik)
    d_att = dt("d_att", [D, QN], BF16, kind=ik)
    d_qmem = dt("d_qmem", [D, QN], BF16, kind=ik)
    d_hid = dt("d_hid", [MH, QN], BF16, kind=ik)
    d_qt = dt("d_qt", [D, QN], BF16)
    d_x1snd = (part == "fused") and dt("d_x1snd", [D, QN], BF16)
    d_x1all = dt("d_x1all", [2 * D, QN], BF16) if part == "fused" else x1in
    outT = L1 and dt("outT", [D, QN], F32, kind=eout)

    with ExitStack() as st0:
        P = Prog(nc, st0)
        sb, ps = P.sbuf, P.psum
        PA = ps(st0, "PA", [128, 512], F32)
        PB = ps(st0, "PB", [128, 512], F32)
        PC = ps(st0, "PC", [128, 512], F32)
        PD = ps(st0, "PD", [128, 512], F32)
        PR = ps(st0, "PR", [128, 512], F32)
        PT = ps(st0, "PT", [128, 1024], BF16)
        PSB = [PA, PB, PC, PD]
        SP = [sb(st0, f"sp{i}", [128, 512], F32) for i in range(5)]
        SBF = [sb(st0, f"sbf{i}", [128, 512], BF16) for i in range(4)]
        WSTG = [sb(st0, f"wstg{i}", [128, 512], F32) for i in range(4)]
        WBF = [sb(st0, f"wbf{i}", [128, 512], BF16) for i in range(4)]
        IDB = sb(st0, "idb", [128, 128], BF16)
        O1F = sb(st0, "o1f", [1, 128], F32)
        OCF = sb(st0, "ocf", [128, 1], F32)
        OCB = sb(st0, "ocb", [128, 1], BF16)
        N1 = sb(st0, "n1s", [128, DT], F32)
        N2 = sb(st0, "n2s", [128, DT], F32)
        SC1 = sb(st0, "sc1s", [128, DT], F32)
        B1 = sb(st0, "b1s", [128, DT], F32)
        QSC = sb(st0, "qscs", [128, 1], F32)
        KSC = sb(st0, "kscs", [128, 1], F32)
        QSS = sb(st0, "qss", [128, 1], F32)
        ROWA = sb(st0, "rowa", [1, 512], F32)
        ROWB = sb(st0, "rowb", [1, 512], F32)
        ROWC = sb(st0, "rowc", [1, 512], F32)
        SR1 = sb(st0, "sr1", [128, 1], F32)
        SR2 = sb(st0, "sr2", [128, 1], F32)
        SR3 = sb(st0, "sr3", [128, 1], F32)
        SR4 = sb(st0, "sr4", [128, 1], F32)
        SRT = sb(st0, "srt", [128, 1], F32)
        RC = sb(st0, "rc", [128, 1], F32)
        RCM = sb(st0, "rcm", [128, 1], F32)
        GT = sb(st0, "gt", [128, 12], F32)
        ID1 = sb(st0, "id1f", [1, 1], F32)
        EPS5 = sb(st0, "eps5", [1, 1], F32)
        EPS6 = sb(st0, "eps6", [1, 1], F32)

        wk = [0]

        def wload(dst_bf16, src_f32, w, p=128):
            stg = WSTG[wk[0] % 4][0:p, 0:w]
            wk[0] += 1
            P.ld(stg, src_f32)
            P.vcopy(dst_bf16, stg)

        P.ld(IDB[:, :], ident[:, :])
        P.ld(O1F[:, :], ones1f[:, :])
        P.ld(OCF[:, :], onescf[:, :])
        P.ld(OCB[:, :], onescb[:, :])
        if L0:
            P.ld(N1[:, :], n1r[:, :])
            P.ld(N2[:, :], n2r[:, :])
        if L1:
            P.ld(SC1[:, :], sc1r[:, :])
            P.ld(B1[:, :], b1r[:, :])
        if L0:
            P.ld(QSC[:, :], qsc[:, :])
            P.ld(KSC[:, :], ksc[:, :])
            P.ts(QSS[:, :], QSC[:, :], SCALE0, ALU.mult)
        P.memset(ID1[:, :], 1.0)
        P.memset(EPS5[:, :], 1e-5)
        P.memset(EPS6[:, :], 1e-6)

        if L0:
            with ExitStack() as s1:
                XNRH = [sb(s1, f"xnrh{t}", [128, KL], BF16) for t in range(DT)]
                ACB = [sb(s1, f"acb{i}", [128, 512], F32) for i in range(2)]
                MROW = sb(s1, "mrow", [1, KL], F32)
                VROW = sb(s1, "vrow", [1, KL], F32)
                R1ROW = sb(s1, "r1row", [1, KL], F32)
                R12ROW = sb(s1, "r12row", [1, KL], F32)
                GROW = sb(s1, "grow", [1, KL], F32)
                for ch in range(3):
                    cs = ch * 512
                    for t in range(DT):
                        xs = SP[t % 2]
                        P.ld(xs[:, :], xT[t * 128:(t + 1) * 128, cs:cs + 512])
                        sq = SP[2 + t % 2]
                        P.tt(sq[:, :], xs[:, :], xs[:, :], ALU.mult)
                        P.mm(PA[0:1, 0:512], OCF[:, :], xs[:, :], t == 0, t == DT - 1)
                        P.mm(PB[0:1, 0:512], OCF[:, :], sq[:, :], t == 0, t == DT - 1)
                    P.scopy(MROW[0:1, cs:cs + 512], PA[0:1, 0:512], scale=1.0 / D)
                    P.scopy(ROWA[:, :], PB[0:1, 0:512], scale=1.0 / D)
                    P.tt(ROWB[:, :], MROW[0:1, cs:cs + 512], MROW[0:1, cs:cs + 512], ALU.mult)
                    P.tt(VROW[0:1, cs:cs + 512], ROWA[:, :], ROWB[:, :], ALU.subtract)
                    P.act(ROWA[:, :], VROW[0:1, cs:cs + 512], AF.Ln, bias=EPS5[:, :])
                    P.act(R1ROW[0:1, cs:cs + 512], ROWA[:, :], AF.Exp, scale=-0.5)
                    P.ts(ROWB[:, :], VROW[0:1, cs:cs + 512], 1e-5, ALU.add)
                    P.recip(ROWC[:, :], ROWB[:, :])
                    P.tt(ROWB[:, :], VROW[0:1, cs:cs + 512], ROWC[:, :], ALU.mult)
                    P.act(ROWC[:, :], ROWB[:, :], AF.Ln, bias=EPS6[:, :])
                    P.act(ROWB[:, :], ROWC[:, :], AF.Exp, scale=-0.5)
                    P.tt(R12ROW[0:1, cs:cs + 512], R1ROW[0:1, cs:cs + 512], ROWB[:, :],
                         ALU.mult)
                for ch in range(3):
                    cs = ch * 512
                    P.mm(PC[:, :], O1F[:, :], MROW[0:1, cs:cs + 512], True, True)
                    P.mm(PD[:, :], O1F[:, :], R1ROW[0:1, cs:cs + 512], True, True)
                    P.mm(PR[:, :], O1F[:, :], R12ROW[0:1, cs:cs + 512], True, True)
                    for t in range(DT):
                        xs = SP[t % 2]
                        P.ld(xs[:, :], xT[t * 128:(t + 1) * 128, cs:cs + 512])
                        t1 = SP[2 + t % 2]
                        P.tt(t1[:, :], xs[:, :], PC[:, :], ALU.subtract)
                        P.stt(XNRH[t][:, cs:cs + 512], t1[:, :], N1[:, t:t + 1], PR[:, :],
                              ALU.mult, ALU.mult)
                        if ch >= 1:
                            xn = SP[4]
                            P.tt(xn[:, :], t1[:, :], PD[:, :], ALU.mult)
                            acb = ACB[t % 2]
                            P.tt(acb[:, :], xs[:, :], xn[:, :], ALU.add)
                            P.st(d_acc[t * 128:(t + 1) * 128, cs - 512:cs], acb[:, :])
                            xb = SBF[t % 2]
                            P.vcopy(xb[:, :], xn[:, :])
                            P.st(d_xnq[t * 128:(t + 1) * 128, cs - 512:cs], xb[:, :])

                with ExitStack() as sB:
                    GG1 = sb(sB, "gg1", [128, DT * 64], BF16)
                    GG2 = sb(sB, "gg2", [64, 1], BF16)
                    GH = sb(sB, "gh", [64, KL], BF16)
                    for t in range(DT):
                        wload(GG1[:, t * 64:(t + 1) * 64],
                              gg_w1[t * 128:(t + 1) * 128, 0:64], 64)
                    wload(GG2[:, :], gg_w2[:, :], 1, p=64)
                    for ch in range(3):
                        cs = ch * 512
                        for t in range(DT):
                            P.mm(PA[0:64, 0:512], GG1[:, t * 64:(t + 1) * 64],
                                 XNRH[t][:, cs:cs + 512], t == 0, t == DT - 1)
                        P.act(GH[:, cs:cs + 512], PA[0:64, 0:512], AF.Silu)
                        P.mm(PB[0:1, 0:512], GG2[:, :], GH[:, cs:cs + 512], True, True)
                        P.act(GROW[0:1, cs:cs + 512], PB[0:1, 0:512], AF.Sigmoid)
                    for kt in range(12):
                        P.trans(PR[:, 0:1], GROW[0:1, kt * 128:(kt + 1) * 128], ID1[:, :])
                        P.scopy(GT[:, kt:kt + 1], PR[:, 0:1])
                    P.barrier()

                MSW = sb(s1, "mswS", [128, 8 * 640], BF16)
                MOM = sb(s1, "momS", [128, 8 * 256], BF16)
                DECT = sb(s1, "dectS", [128, 256], BF16)
                CS0 = sb(s1, "cs0", [128, KL], F32)
                SN0 = sb(s1, "sn0", [128, KL], F32)
                for qt in range(8):
                    P.ld(MSW[:, qt * 640:(qt + 1) * 640], msw[qt * 128:(qt + 1) * 128, :])
                    P.ld(MOM[:, qt * 256:(qt + 1) * 256], mom[qt * 128:(qt + 1) * 128, :])
                P.ld(DECT[:, :], dect[:, :])
                P.ld(CS0[:, :], cos0[:, :])
                P.ld(SN0[:, :], sin0[:, :])
                WQH = sb(s1, "wqh", [128, D], BF16)
                WKH = sb(s1, "wkh", [128, D], BF16)
                WVH = sb(s1, "wvh", [128, D], BF16)
                QTH = sb(s1, "qth", [128, QN], BF16)
                KTH = sb(s1, "kth", [128, KL], BF16)
                KNAT = sb(s1, "knat", [128, 9 * 128], BF16)
                VNAT = sb(s1, "vnat", [128, KL], BF16)
                PSW = sb(s1, "psw", [128, 640], BF16)
                POM = sb(s1, "pom", [128, 256], BF16)
                PTS = sb(s1, "pts", [128, 640], BF16)
                PTSM = sb(s1, "ptsm", [128, 256], BF16)
                ATMP = sb(s1, "atmp", [128, 128], BF16)
                OST = [sb(s1, f"ost{i}", [128, 128], BF16) for i in range(4)]

                def qk_process(psum_ap, cols_tab, scol, out_ap):
                    q0 = SP[0]
                    P.scopy(q0[:, :], psum_ap)
                    sqb = SBF[0]
                    P.act(sqb[:, :], psum_ap, AF.Square)
                    P.mm(PB[0:1, 0:512], OCB[:, :], sqb[:, :], True, True)
                    P.act(ROWA[:, :], PB[0:1, 0:512], AF.Ln, scale=1.0 / 128, bias=EPS6[:, :])
                    P.act(ROWB[:, :], ROWA[:, :], AF.Exp, scale=-0.5)
                    P.mm(PC[:, :], O1F[:, :], ROWB[:, :], True, True)
                    qn_ = SP[1]
                    P.stt(qn_[:, :], q0[:, :], scol, PC[:, :], ALU.mult, ALU.mult)
                    sh = SP[2]
                    P.vcopy(sh[0:64, :], qn_[64:128, :])
                    P.vcopy(sh[64:128, :], qn_[0:64, :])
                    a_ = SP[3]
                    P.tt(a_[:, :], qn_[:, :], CS0[:, cols_tab:cols_tab + 512], ALU.mult)
                    b_ = SP[4]
                    P.tt(b_[:, :], sh[:, :], SN0[:, cols_tab:cols_tab + 512], ALU.mult)
                    P.tt(out_ap, a_[:, :], b_[:, :], ALU.add)

                for h in range(H0):
                    for t in range(DT):
                        r = slice(t * 128, (t + 1) * 128)
                        c = slice(t * 128, (t + 1) * 128)
                        wload(WQH[:, c], w_qkv[r, h * 128:(h + 1) * 128], 128)
                        wload(WKH[:, c], w_qkv[r, D + h * 128:D + (h + 1) * 128], 128)
                        wload(WVH[:, c], w_qkv[r, 2 * D + h * 128:2 * D + (h + 1) * 128], 128)
                    for c2 in range(2):
                        cs = 512 + c2 * 512
                        for t in range(DT):
                            P.mm(PA[:, :], WQH[:, t * 128:(t + 1) * 128],
                                 XNRH[t][:, cs:cs + 512], t == 0, t == DT - 1)
                        qk_process(PA[:, :], cs, QSS[:, :], QTH[:, c2 * 512:c2 * 512 + 512])
                    for ch in range(3):
                        cs = ch * 512
                        for t in range(DT):
                            P.mm(PA[:, :], WKH[:, t * 128:(t + 1) * 128],
                                 XNRH[t][:, cs:cs + 512], t == 0, t == DT - 1)
                        qk_process(PA[:, :], cs, KSC[:, :], KTH[:, cs:cs + 512])
                        for t in range(DT):
                            P.mm(PA[:, :], WVH[:, t * 128:(t + 1) * 128],
                                 XNRH[t][:, cs:cs + 512], t == 0, t == DT - 1)
                        vtb = SBF[2]
                        P.vcopy(vtb[:, :], PA[:, :])
                        for j in range(4):
                            kt = ch * 4 + j
                            sl = PT[:, (kt % 8) * 128:(kt % 8) * 128 + 128]
                            P.trans(sl, vtb[:, j * 128:(j + 1) * 128], IDB[:, :])
                            P.vcopy(VNAT[:, kt * 128:(kt + 1) * 128], sl)
                    for kt in range(3, 12):
                        sl = PT[:, (kt % 8) * 128:(kt % 8) * 128 + 128]
                        P.trans(sl, KTH[:, kt * 128:(kt + 1) * 128], IDB[:, :])
                        P.vcopy(KNAT[:, (kt - 3) * 128:(kt - 2) * 128], sl)
                    for qt in range(8):
                        qsl = QTH[:, qt * 128:(qt + 1) * 128]
                        k0 = qt * 128
                        P.mm(PA[:, :], qsl, KTH[:, k0:k0 + 512], True, False)
                        P.mm(PA[:, :], IDB[:, :], MSW[:, qt * 640:qt * 640 + 512], False, True)
                        P.mm(PB[:, 0:128], qsl, KTH[:, k0 + 512:k0 + 640], True, False)
                        P.mm(PB[:, 0:128], IDB[:, :],
                             MSW[:, qt * 640 + 512:qt * 640 + 640], False, True)
                        P.act(PSW[:, 0:512], PA[:, :], AF.Exp, accum_out=SR1[:, :])
                        P.act(PSW[:, 512:640], PB[:, 0:128], AF.Exp, accum_out=SR2[:, :])
                        P.tt(SRT[:, :], SR1[:, :], SR2[:, :], ALU.add)
                        P.recip(RC[:, :], SRT[:, :])
                        for j in range(5):
                            sl = PT[:, (j % 8) * 128:(j % 8) * 128 + 128]
                            P.trans(sl, PSW[:, j * 128:(j + 1) * 128], IDB[:, :])
                            P.vcopy(PTS[:, j * 128:(j + 1) * 128], sl)
                        for j in range(5):
                            P.mm(PC[:, 0:128], PTS[:, j * 128:(j + 1) * 128],
                                 VNAT[:, (qt + j) * 128:(qt + j + 1) * 128], j == 0, j == 4)
                        P.act(ATMP[:, :], PC[:, 0:128], AF.Copy, scale=RC[:, :])
                        P.trans(PT[:, 640:768], ATMP[:, :], IDB[:, :])
                        P.vcopy(OST[qt % 2][:, :], PT[:, 640:768])
                        P.st(d_att[h * 128:(h + 1) * 128, qt * 128:(qt + 1) * 128],
                             OST[qt % 2][:, :])
                        P.mm(PD[:, 0:256], qsl, KTH[:, k0 + 384:k0 + 640], True, False)
                        P.mm(PD[:, 0:256], IDB[:, :], MOM[:, qt * 256:(qt + 1) * 256],
                             False, True)
                        P.act(POM[:, :], PD[:, 0:256], AF.Exp, accum_out=SR3[:, :])
                        P.recip(RCM[:, :], SR3[:, :])
                        for j in range(2):
                            sl = PT[:, 768 + j * 128:768 + (j + 1) * 128]
                            P.trans(sl, POM[:, j * 128:(j + 1) * 128], IDB[:, :])
                            P.stt(PTSM[:, j * 128:(j + 1) * 128], sl,
                                  GT[:, qt + 3 + j:qt + 4 + j],
                                  DECT[:, j * 128:(j + 1) * 128], ALU.mult, ALU.mult)
                        for j in range(2):
                            P.mm(PR[:, 0:128], PTSM[:, j * 128:(j + 1) * 128],
                                 KNAT[:, (qt + j) * 128:(qt + j + 1) * 128], j == 0, j == 1)
                        P.act(ATMP[:, :], PR[:, 0:128], AF.Copy, scale=RCM[:, :])
                        P.trans(PT[:, 640:768], ATMP[:, :], IDB[:, :])
                        P.vcopy(OST[2 + qt % 2][:, :], PT[:, 640:768])
                        P.st(d_qmem[h * 128:(h + 1) * 128, qt * 128:(qt + 1) * 128],
                             OST[2 + qt % 2][:, :])
                P.barrier()

        ACC = [sb(st0, f"acc{t}", [128, QN], F32) for t in range(DT)]
        for t in range(DT):
            P.ld(ACC[t][:, :], (d_acc if L0 else accin)[t * 128:(t + 1) * 128, :])

        def contract_back(w_dram, ktiles, rhs_dram, epilogue):
            with ExitStack() as s4:
                HAR = [sb(s4, f"cbh{i}", [128, 512], BF16) for i in range(2)]
                nk = len(ktiles)
                for tch in range(2):
                    cs = tch * 512
                    for dg in range(4):
                        for ki, (r0, rw) in enumerate(ktiles):
                            har = HAR[ki % 2]
                            P.ld(har[0:rw, :], rhs_dram[r0:r0 + rw, cs:cs + 512])
                            wc = WBF[ki % 4]
                            wload(wc[0:rw, :], w_dram[r0:r0 + rw, dg * 512:dg * 512 + 512],
                                  512, p=rw)
                            for dd in range(4):
                                P.mm(PSB[dd][:, :], wc[0:rw, dd * 128:(dd + 1) * 128],
                                     har[0:rw, :], ki == 0, ki == nk - 1)
                        for dd in range(4):
                            epilogue(dg * 4 + dd, cs, PSB[dd])
            P.barrier()

        if L0:
            with ExitStack() as s2:
                PROJ = [sb(s2, f"proj{t}", [128, QN], BF16) for t in range(DT)]

                def ep_proj(do, cs, pp):
                    P.vcopy(PROJ[do][:, cs:cs + 512], pp[:, :])
                contract_back(w_o, [(t * 128, 128) for t in range(DT)], d_att, ep_proj)

                with ExitStack() as s3:
                    QS2 = [sb(s3, f"qs2_{i}", [128, QN], BF16) for i in range(2)]
                    HBs = [sb(s3, f"hbe{i}", [128, 512], BF16) for i in range(4)]
                    for hg in range(32):
                        for t in range(DT):
                            qs_ = QS2[t % 2]
                            P.ld(qs_[:, :], d_qmem[t * 128:(t + 1) * 128, :])
                            wc = WBF[t % 4]
                            wload(wc[:, 0:256],
                                  mem_w1[t * 128:(t + 1) * 128, hg * 256:hg * 256 + 256], 256)
                            for h2 in range(2):
                                for tch in range(2):
                                    P.mm(PSB[h2 * 2 + tch][:, :],
                                         wc[:, h2 * 128:(h2 + 1) * 128],
                                         qs_[:, tch * 512:tch * 512 + 512],
                                         t == 0, t == DT - 1)
                        for h2 in range(2):
                            hh = hg * 2 + h2
                            for tch in range(2):
                                pp = PSB[h2 * 2 + tch]
                                phi = SP[h2 * 2 + tch]
                                P.act(phi[:, :], pp[:, :], AF.Silu)
                                sq = SP[4]
                                P.tt(sq[:, :], phi[:, :], phi[:, :], ALU.mult)
                                hb = HBs[h2 * 2 + tch]
                                P.tt(hb[:, :], phi[:, :], sq[:, :], ALU.add)
                                P.st(d_hid[hh * 128:(hh + 1) * 128,
                                           tch * 512:tch * 512 + 512], hb[:, :])
                P.barrier()

                with ExitStack() as s3:
                    XQB = [sb(s3, f"xqb{i}", [128, 512], BF16) for i in range(2)]
                    A1B = [sb(s3, f"a1b{i}", [128, 512], BF16) for i in range(2)]

                    def ep_mem(do, cs, pp):
                        sg = SP[0]
                        P.act(sg[:, :], pp[:, :], AF.Sigmoid)
                        tmp = SP[1]
                        P.tt(tmp[:, :], PROJ[do][:, cs:cs + 512], sg[:, :], ALU.mult)
                        P.tt(ACC[do][:, cs:cs + 512], ACC[do][:, cs:cs + 512], tmp[:, :],
                             ALU.add)
                        xq = XQB[do % 2]
                        P.ld(xq[:, :], d_xnq[do * 128:(do + 1) * 128, cs:cs + 512])
                        a1b = A1B[do % 2]
                        P.tt(a1b[:, :], xq[:, :], tmp[:, :], ALU.add)
                        P.st(d_att[do * 128:(do + 1) * 128, cs:cs + 512], a1b[:, :])
                    contract_back(mem_w2, [(k * 128, 128) for k in range(64)], d_hid, ep_mem)

            with ExitStack() as s2:
                YT = [sb(s2, f"yt{t}", [128, QN], BF16) for t in range(DT)]
                A1S = [sb(s2, f"a1s{i}", [128, 512], BF16) for i in range(2)]
                R3ROW = sb(s2, "r3row", [1, QN], F32)
                for ch in range(2):
                    cs = ch * 512
                    for t in range(DT):
                        a1 = A1S[t % 2]
                        P.ld(a1[:, :], d_att[t * 128:(t + 1) * 128, cs:cs + 512])
                        sq = SBF[t % 2]
                        P.tt(sq[:, :], a1[:, :], a1[:, :], ALU.mult)
                        P.mm(PB[0:1, 0:512], OCB[:, :], sq[:, :], t == 0, t == DT - 1)
                    P.act(ROWA[:, :], PB[0:1, 0:512], AF.Ln, scale=1.0 / D, bias=EPS6[:, :])
                    P.act(R3ROW[0:1, cs:cs + 512], ROWA[:, :], AF.Exp, scale=-0.5)
                for ch in range(2):
                    cs = ch * 512
                    P.mm(PR[:, :], O1F[:, :], R3ROW[0:1, cs:cs + 512], True, True)
                    for t in range(DT):
                        a1 = A1S[t % 2]
                        P.ld(a1[:, :], d_att[t * 128:(t + 1) * 128, cs:cs + 512])
                        P.stt(YT[t][:, cs:cs + 512], a1[:, :], N2[:, t:t + 1], PR[:, :],
                              ALU.mult, ALU.mult)
                fts = [(i * 128, 128) for i in range(43)]
                HBs = [sb(s2, f"hbf{i}", [128, 512], BF16) for i in range(4)]
                GLD = [sb(s2, f"gld{i}", [128, 512], BF16) for i in range(2)]
                for pas in range(2):
                    wsrc = ffn_w1 if pas == 0 else ffn_w3
                    for gi in range(22):
                        grp = fts[gi * 2:gi * 2 + 2]
                        for t in range(DT):
                            wc = WBF[t % 4]
                            c0 = grp[0][0]
                            wtot = sum(w for (_, w) in grp)
                            wload(wc[:, 0:wtot],
                                  wsrc[t * 128:(t + 1) * 128, c0:c0 + wtot], wtot)
                            for fi, (fo, fw) in enumerate(grp):
                                for tch in range(2):
                                    P.mm(PSB[fi * 2 + tch][0:fw, :],
                                         wc[:, fi * 128:fi * 128 + fw],
                                         YT[t][:, tch * 512:tch * 512 + 512],
                                         t == 0, t == DT - 1)
                        for fi, (fo, fw) in enumerate(grp):
                            for tch in range(2):
                                pp = PSB[fi * 2 + tch]
                                hb = HBs[fi * 2 + tch]
                                if pas == 0:
                                    P.act(hb[0:fw, :], pp[0:fw, :], AF.Silu)
                                else:
                                    gl = GLD[fi]
                                    P.ld(gl[0:fw, :],
                                         d_hid[fo:fo + fw, tch * 512:tch * 512 + 512])
                                    P.tt(hb[0:fw, :], gl[0:fw, :], pp[0:fw, :], ALU.mult)
                                P.st(d_hid[fo:fo + fw, tch * 512:tch * 512 + 512],
                                     hb[0:fw, :])
                    P.barrier()

                def ep_acc(do, cs, pp):
                    P.tt(ACC[do][:, cs:cs + 512], ACC[do][:, cs:cs + 512], pp[:, :], ALU.add)
                contract_back(ffn_w2, fts, d_hid, ep_acc)

            with ExitStack() as s2:
                XNQR = [sb(s2, f"xnqr{t}", [128, QN], BF16) for t in range(DT)]
                for t in range(DT):
                    P.ld(XNQR[t][:, :], d_xnq[t * 128:(t + 1) * 128, :])
                HBs = [sb(s2, f"hbg{i}", [128, 512], BF16) for i in range(4)]
                GLD = [sb(s2, f"gldg{i}", [128, 512], BF16) for i in range(2)]
                for pas in range(2):
                    wsrc = wg_t if pas == 0 else wu_t
                    for hg in range(32):
                        for t in range(DT):
                            wc = WBF[t % 4]
                            P.ld(WSTG[t % 4][:, 0:256],
                                 wsrc[t * 128:(t + 1) * 128, hg * 256:hg * 256 + 256])
                            P.vcopy(wc[:, 0:256], WSTG[t % 4][:, 0:256])
                            for h2 in range(2):
                                for tch in range(2):
                                    P.mm(PSB[h2 * 2 + tch][:, :],
                                         wc[:, h2 * 128:(h2 + 1) * 128],
                                         XNQR[t][:, tch * 512:tch * 512 + 512],
                                         t == 0, t == DT - 1)
                        for h2 in range(2):
                            hh = hg * 2 + h2
                            for tch in range(2):
                                pp = PSB[h2 * 2 + tch]
                                hb = HBs[h2 * 2 + tch]
                                if pas == 0:
                                    P.act(hb[:, :], pp[:, :], AF.Silu)
                                else:
                                    gl = GLD[h2]
                                    P.ld(gl[:, :],
                                         d_hid[hh * 128:(hh + 1) * 128,
                                               tch * 512:tch * 512 + 512])
                                    P.tt(hb[:, :], gl[:, :], pp[:, :], ALU.mult)
                                P.st(d_hid[hh * 128:(hh + 1) * 128,
                                           tch * 512:tch * 512 + 512], hb[:, :])
                    P.barrier()

                def ep_acc2(do, cs, pp):
                    P.tt(ACC[do][:, cs:cs + 512], ACC[do][:, cs:cs + 512], pp[:, :], ALU.add)
                contract_back(wd_t, [(k * 128, 128) for k in range(64)], d_hid, ep_acc2)

        if part != "l1":
            XB = [sb(st0, f"xb1_{i}", [128, QN], BF16) for i in range(2)]
            x1dst = d_x1snd if part == "fused" else x1out
            for t in range(DT):
                xb = XB[t % 2]
                P.vcopy(xb[:, :], ACC[t][:, :])
                P.st(x1dst[t * 128:(t + 1) * 128, :], xb[:, :])
            if part == "l0":
                for t in range(DT):
                    P.st(accout[t * 128:(t + 1) * 128, :], ACC[t][:, :])
            P.barrier()
        if part == "fused":
            P.collective("AllGather", ALU.bypass, [[0, 1], [2, 3], [4, 5], [6, 7]],
                         d_x1snd.ap().opt(), d_x1all.ap().opt())
            P.barrier()

        if L1:
            with ExitStack() as s2:
                KT1 = [sb(s2, f"kt1_{j}", [128, S], BF16) for j in range(4)]
                VN1 = [sb(s2, f"vn1_{k}", [128, 512], BF16) for k in range(16)]
                with ExitStack() as s3:
                    CS1K = sb(s3, "cs1k", [128, S], F32)
                    SN1K = sb(s3, "sn1k", [128, S], F32)
                    P.ld(CS1K[:, :], cos1k[:, :])
                    P.ld(SN1K[:, :], sin1k[:, :])
                    X1A = [sb(s3, f"x1a{t}", [128, 512], BF16) for t in range(DT)]
                    WVB = [sb(s3, f"wvb{t}", [128, 512], BF16) for t in range(DT)]
                    for t in range(DT):
                        wload(WVB[t][:, :], wv[t * 128:(t + 1) * 128, :], 512)
                    for tc in range(4):
                        rb = (tc // 2) * D
                        cs = (tc % 2) * 512
                        for t in range(DT):
                            P.ld(X1A[t][:, :],
                                 d_x1all[rb + t * 128:rb + (t + 1) * 128, cs:cs + 512])
                            sq = SBF[t % 2]
                            P.tt(sq[:, :], X1A[t][:, :], X1A[t][:, :], ALU.mult)
                            P.mm(PA[0:1, 0:512], OCB[:, :], X1A[t][:, :], t == 0, t == DT - 1)
                            P.mm(PB[0:1, 0:512], OCB[:, :], sq[:, :], t == 0, t == DT - 1)
                        P.scopy(ROWA[:, :], PA[0:1, 0:512], scale=1.0 / D)
                        P.scopy(ROWB[:, :], PB[0:1, 0:512], scale=1.0 / D)
                        P.tt(ROWC[:, :], ROWA[:, :], ROWA[:, :], ALU.mult)
                        P.tt(ROWB[:, :], ROWB[:, :], ROWC[:, :], ALU.subtract)
                        P.act(ROWC[:, :], ROWB[:, :], AF.Ln, bias=EPS5[:, :])
                        P.act(ROWB[:, :], ROWC[:, :], AF.Exp, scale=-0.5)
                        P.mm(PC[:, :], O1F[:, :], ROWA[:, :], True, True)
                        P.mm(PD[:, :], O1F[:, :], ROWB[:, :], True, True)
                        for t in range(DT):
                            t1 = SP[t % 2]
                            P.tt(t1[:, :], X1A[t][:, :], PC[:, :], ALU.subtract)
                            P.tt(X1A[t][:, :], t1[:, :], PD[:, :], ALU.mult)
                        for jg in range(2):
                            for t in range(DT):
                                wc = WBF[t % 4]
                                wload(wc[:, 0:256],
                                      wk_p[t * 128:(t + 1) * 128,
                                           jg * 256:jg * 256 + 256], 256)
                                for j2 in range(2):
                                    P.mm(PSB[j2][:, :], wc[:, j2 * 128:(j2 + 1) * 128],
                                         X1A[t][:, :], t == 0, t == DT - 1)
                            for j2 in range(2):
                                jj = jg * 2 + j2
                                kr = SP[0]
                                P.scopy(kr[:, :], PSB[j2][:, :])
                                sh = SP[1]
                                for bs in (0, 64):
                                    P.vcopy(sh[bs:bs + 32, :], kr[bs + 32:bs + 64, :])
                                    P.vcopy(sh[bs + 32:bs + 64, :], kr[bs:bs + 32, :])
                                a_ = SP[2]
                                P.tt(a_[:, :], kr[:, :], CS1K[:, tc * 512:tc * 512 + 512],
                                     ALU.mult)
                                b_ = SP[3]
                                P.tt(b_[:, :], sh[:, :], SN1K[:, tc * 512:tc * 512 + 512],
                                     ALU.mult)
                                P.tt(KT1[jj][:, tc * 512:tc * 512 + 512], a_[:, :], b_[:, :],
                                     ALU.add)
                        for s4i in range(4):
                            for t in range(DT):
                                P.mm(PR[:, :], X1A[t][:, s4i * 128:(s4i + 1) * 128],
                                     WVB[t][:, :], t == 0, t == DT - 1)
                            P.vcopy(VN1[tc * 4 + s4i][:, :], PR[:, :])
                P.barrier()

                with ExitStack() as s3:
                    XN2 = [sb(s3, f"xn2_{t}", [128, QN], BF16) for t in range(DT)]
                    for ch in range(2):
                        cs = ch * 512
                        for t in range(DT):
                            sq = SP[t % 2]
                            P.tt(sq[:, :], ACC[t][:, cs:cs + 512], ACC[t][:, cs:cs + 512],
                                 ALU.mult)
                            P.mm(PA[0:1, 0:512], OCF[:, :], ACC[t][:, cs:cs + 512],
                                 t == 0, t == DT - 1)
                            P.mm(PB[0:1, 0:512], OCF[:, :], sq[:, :], t == 0, t == DT - 1)
                        P.scopy(ROWA[:, :], PA[0:1, 0:512], scale=1.0 / D)
                        P.scopy(ROWB[:, :], PB[0:1, 0:512], scale=1.0 / D)
                        P.tt(ROWC[:, :], ROWA[:, :], ROWA[:, :], ALU.mult)
                        P.tt(ROWB[:, :], ROWB[:, :], ROWC[:, :], ALU.subtract)
                        P.act(ROWC[:, :], ROWB[:, :], AF.Ln, bias=EPS5[:, :])
                        P.act(ROWB[:, :], ROWC[:, :], AF.Exp, scale=-0.5)
                        P.mm(PC[:, :], O1F[:, :], ROWA[:, :], True, True)
                        P.mm(PD[:, :], O1F[:, :], ROWB[:, :], True, True)
                        for t in range(DT):
                            t1 = SP[t % 2]
                            P.tt(t1[:, :], ACC[t][:, cs:cs + 512], PC[:, :], ALU.subtract)
                            P.tt(XN2[t][:, cs:cs + 512], t1[:, :], PD[:, :], ALU.mult)
                    CS1Q = sb(s3, "cs1q", [128, QN], F32)
                    SN1Q = sb(s3, "sn1q", [128, QN], F32)
                    P.ld(CS1Q[:, :], cos1q[:, :])
                    P.ld(SN1Q[:, :], sin1q[:, :])
                    QOB = [sb(s3, f"qob{i}", [128, 512], BF16) for i in range(2)]
                    for mg in range(8):
                        for t in range(DT):
                            wc = WBF[t % 4]
                            wload(wc[:, 0:256],
                                  wq_p2[t * 128:(t + 1) * 128, mg * 256:mg * 256 + 256], 256)
                            for m2 in range(2):
                                for tch in range(2):
                                    P.mm(PSB[m2 * 2 + tch][:, :],
                                         wc[:, m2 * 128:(m2 + 1) * 128],
                                         XN2[t][:, tch * 512:tch * 512 + 512],
                                         t == 0, t == DT - 1)
                        for m2 in range(2):
                            m = mg * 2 + m2
                            for tch in range(2):
                                cs = tch * 512
                                pp = PSB[m2 * 2 + tch]
                                qr = SP[0]
                                P.scopy(qr[:, :], pp[:, :])
                                sh = SP[1]
                                for bs in (0, 64):
                                    P.vcopy(sh[bs:bs + 32, :], qr[bs + 32:bs + 64, :])
                                    P.vcopy(sh[bs + 32:bs + 64, :], qr[bs:bs + 32, :])
                                a_ = SP[2]
                                P.tt(a_[:, :], qr[:, :], CS1Q[:, cs:cs + 512], ALU.mult)
                                b_ = SP[3]
                                P.tt(b_[:, :], sh[:, :], SN1Q[:, cs:cs + 512], ALU.mult)
                                qo = QOB[m2]
                                P.tt(qo[:, :], a_[:, :], b_[:, :], ALU.add)
                                P.st(d_qt[m * 128:(m + 1) * 128, cs:cs + 512], qo[:, :])
                    for tch in range(2):
                        cs = tch * 512
                        for dg in range(4):
                            for t in range(DT):
                                wc = WBF[t % 4]
                                wload(wc[:, :],
                                      w1f_t[t * 128:(t + 1) * 128, dg * 512:dg * 512 + 512],
                                      512)
                                for dd in range(4):
                                    P.mm(PSB[dd][:, :], wc[:, dd * 128:(dd + 1) * 128],
                                         XN2[t][:, cs:cs + 512], t == 0, t == DT - 1)
                            for dd in range(4):
                                do = dg * 4 + dd
                                tmp = SP[0]
                                P.stt(tmp[:, :], PSB[dd][:, :], SC1[:, do:do + 1],
                                      ACC[do][:, cs:cs + 512], ALU.mult, ALU.add)
                                P.ts(ACC[do][:, cs:cs + 512], tmp[:, :], B1[:, do:do + 1],
                                     ALU.add)
                P.barrier()

                with ExitStack() as s3:
                    M1B = [sb(s3, f"m1b{i}", [128, S], BF16) for i in range(2)]
                    QTQ = sb(s3, "qtq", [128, D], BF16)
                    P1 = sb(s3, "p1", [128, S], BF16)
                    PTSL = sb(s3, "ptsl", [128, S], BF16)
                    ATTQ = sb(s3, "attq", [128, D], BF16)
                    ATMP2 = sb(s3, "atmp2", [128, 64], BF16)
                    for qt in range(8):
                        mb = M1B[qt % 2]
                        P.ld(mb[:, :], m1[qt * 128:(qt + 1) * 128, :])
                        for m in range(DT):
                            P.ld(QTQ[:, m * 128:(m + 1) * 128],
                                 d_qt[m * 128:(m + 1) * 128, qt * 128:(qt + 1) * 128])
                        for m in range(DT):
                            for slot in range(2):
                                g = EV_HEADS[m] if slot == 0 else OD_HEADS[m]
                                j = g // 4
                                jj, half = j // 2, (j % 2) * 64
                                bs = slot * 64
                                assert half == bs, (g, j, m, slot)
                                qls = QTQ[bs:bs + 64, m * 128:(m + 1) * 128]
                                for kc in range(4):
                                    P.mm(PSB[kc][:, :], qls,
                                         KT1[jj][bs:bs + 64, kc * 512:kc * 512 + 512],
                                         True, False)
                                    P.mm(PSB[kc][:, :], IDB[:, :],
                                         mb[:, kc * 512:kc * 512 + 512], False, True)
                                for kc in range(4):
                                    P.act(P1[:, kc * 512:kc * 512 + 512], PSB[kc][:, :],
                                          AF.Exp, accum_out=[SR1, SR2, SR3, SR4][kc][:, :])
                                P.tt(SRT[:, :], SR1[:, :], SR2[:, :], ALU.add)
                                P.tt(RCM[:, :], SR3[:, :], SR4[:, :], ALU.add)
                                P.tt(SRT[:, :], SRT[:, :], RCM[:, :], ALU.add)
                                P.recip(RC[:, :], SRT[:, :])
                                for kt in range(16):
                                    sl = PT[:, (kt % 8) * 128:(kt % 8) * 128 + 128]
                                    P.trans(sl, P1[:, kt * 128:(kt + 1) * 128], IDB[:, :])
                                    P.vcopy(PTSL[:, kt * 128:(kt + 1) * 128], sl)
                                for kt in range(16):
                                    P.mm(PR[:, 0:64], PTSL[:, kt * 128:(kt + 1) * 128],
                                         VN1[kt][:, j * 64:(j + 1) * 64], kt == 0, kt == 15)
                                P.act(ATMP2[:, :], PR[:, 0:64], AF.Copy, scale=RC[:, :])
                                P.trans(PT[0:64, 896:1024], ATMP2[:, :], IDB[:, :])
                                P.vcopy(ATTQ[bs:bs + 64, m * 128:(m + 1) * 128],
                                        PT[0:64, 896:1024])
                        for m in range(DT):
                            P.st(d_att[m * 128:(m + 1) * 128, qt * 128:(qt + 1) * 128],
                                 ATTQ[:, m * 128:(m + 1) * 128])
                P.barrier()

                def ep_acc3(do, cs, pp):
                    P.tt(ACC[do][:, cs:cs + 512], ACC[do][:, cs:cs + 512], pp[:, :], ALU.add)
                contract_back(wo_p, [(t * 128, 128) for t in range(DT)], d_att, ep_acc3)

            for t in range(DT):
                P.st(outT[t * 128:(t + 1) * 128, :], ACC[t][:, :])
        P.barrier()
        P.build()
    return nc


EV_HEADS = [g for g in range(32) if (g // 4) % 2 == 0]
OD_HEADS = [g for g in range(32) if (g // 4) % 2 == 1]


def _perm64():
    p = np.zeros(64, dtype=np.int64)
    p[0:32] = np.arange(0, 64, 2)
    p[32:64] = np.arange(1, 64, 2)
    return p


def _padc(a, n):
    out = np.zeros((a.shape[0], n), a.dtype)
    out[:, :a.shape[1]] = a
    return out


def _padr(a, n):
    out = np.zeros((n, a.shape[1]), a.dtype)
    out[:a.shape[0], :] = a
    return out


def _stage(inputs):
    f32 = np.float32
    x = np.asarray(inputs["L0_attn_x"], dtype=f32)
    perm = _perm64()
    wq = np.asarray(inputs["L1_attn_wq"], f32)
    wk = np.asarray(inputs["L1_attn_wk"], f32)
    q_col = np.concatenate([np.concatenate([e * 64 + perm, o * 64 + perm])
                            for e, o in zip(EV_HEADS, OD_HEADS)])
    wq_p2 = np.ascontiguousarray(wq[:, q_col])
    k_col = np.concatenate([j * 64 + perm for j in range(8)])
    wk_p = np.ascontiguousarray(wk[:, k_col])
    wo_row = np.concatenate([np.concatenate([e * 64 + np.arange(64),
                                             o * 64 + np.arange(64)])
                             for e, o in zip(EV_HEADS, OD_HEADS)])
    wo_p = np.ascontiguousarray(np.asarray(inputs["L1_attn_wo"], f32)[wo_row, :])

    shared = {
        "w_qkv": np.asarray(inputs["L0_attn_w_qkv"], f32),
        "gg_w1": np.asarray(inputs["L0_attn_gg_w1"], f32),
        "gg_w2": np.asarray(inputs["L0_attn_gg_w2"], f32),
        "w_o": np.asarray(inputs["L0_attn_w_o"], f32),
        "mem_w1": np.asarray(inputs["L0_attn_mem_w1"], f32),
        "mem_w2": np.asarray(inputs["L0_attn_mem_w2"], f32),
        "ffn_w1": _padc(np.asarray(inputs["L0_attn_ffn_w1"], f32), FH),
        "ffn_w3": _padc(np.asarray(inputs["L0_attn_ffn_w3"], f32), FH),
        "ffn_w2": _padr(np.asarray(inputs["L0_attn_ffn_w2"], f32), FH),
        "wg_t": np.ascontiguousarray(np.asarray(inputs["L0_ffn_w_gate"], f32).T),
        "wu_t": np.ascontiguousarray(np.asarray(inputs["L0_ffn_w_up"], f32).T),
        "wd_t": np.ascontiguousarray(np.asarray(inputs["L0_ffn_w_down"], f32).T),
        "wq_p2": wq_p2,
        "wk_p": wk_p,
        "wv": np.asarray(inputs["L1_attn_wv"], f32),
        "wo_p": wo_p,
        "w1f_t": np.ascontiguousarray(np.asarray(inputs["L1_ffn_weight"], f32).T),
        "identI": np.eye(128, dtype=BFD),
        "ones1f": np.ones((1, 128), f32),
        "onescf": np.ones((128, 1), f32),
        "onescb": np.ones((128, 1), BFD),
        "n1r": np.ascontiguousarray(
            np.asarray(inputs["L0_attn_norm1_w"], f32).reshape(DT, 128).T),
        "n2r": np.ascontiguousarray(
            np.asarray(inputs["L0_attn_norm2_w"], f32).reshape(DT, 128).T),
        "sc1r": np.ascontiguousarray(
            np.asarray(inputs["L1_ffn_scale_weight"], f32).reshape(DT, 128).T),
        "b1r": np.ascontiguousarray(
            np.asarray(inputs["L1_ffn_bias"], f32).reshape(DT, 128).T),
        "qsc": np.asarray(inputs["L0_attn_q_scale"], f32).reshape(128, 1),
        "ksc": np.asarray(inputs["L0_attn_k_scale"], f32).reshape(128, 1),
    }
    invf0 = 10000.0 ** (-np.arange(64, dtype=np.float64) / 64.0)
    invf1 = 10000.0 ** (-np.arange(32, dtype=np.float64) / 32.0)
    pos1 = np.arange(S, dtype=np.float64)
    ang1 = pos1[None, :] * invf1[:, None]
    c1 = np.cos(ang1)
    s1_ = np.sin(ang1)
    cos1_64 = np.concatenate([c1, c1], axis=0)
    sin1_64 = np.concatenate([-s1_, s1_], axis=0)
    cos1k = np.concatenate([cos1_64, cos1_64], axis=0).astype(f32)
    sin1k = np.concatenate([sin1_64, sin1_64], axis=0).astype(f32)
    ii = np.arange(128)
    kk = np.arange(128)
    dect = np.zeros((128, 256), np.float64)
    for j in range(2):
        rel = ii[None, :] - kk[:, None] - j * 128 + 128
        v = np.where((rel >= 0) & (rel < 64), 0.9 ** rel, 0.0)
        dect[:, j * 128:(j + 1) * 128] = v
    shared["dect"] = dect.astype(BFD)
    shared["cos1k"] = cos1k
    shared["sin1k"] = sin1k

    per_core = []
    for c in range(8):
        b, qh = c // 2, c % 2
        kstart = qh * 1024 - 512
        qstart = qh * 1024
        xTc = np.zeros((D, KL), f32)
        lo = max(kstart, 0)
        xTc[:, lo - kstart:] = x[b, lo:kstart + KL, :].T
        jj640 = np.arange(640)
        msk = np.zeros((8, 128, 640), np.float64)
        momk = np.zeros((8, 128, 256), np.float64)
        for qt in range(8):
            rel = ii[:, None] - jj640[None, :] + 512
            real = (kstart + qt * 128 + jj640)[None, :] >= 0
            ok = (rel >= 0) & (rel <= 511) & real
            msk[qt] = np.where(ok, 0.0, NEG)
            jj256 = np.arange(256)
            rel2 = ii[:, None] - jj256[None, :] + 128
            real2 = (kstart + qt * 128 + 384 + jj256)[None, :] >= 0
            ok2 = (rel2 >= 0) & (rel2 < 64) & real2
            momk[qt] = np.where(ok2, 0.0, NEG)
        m1c = np.zeros((8, 128, S), np.float64)
        jS = np.arange(S)
        for qt in range(8):
            qg = qstart + qt * 128 + ii
            m1c[qt] = np.where(jS[None, :] <= qg[:, None], 0.0, NEG)
        pos0 = np.maximum(kstart + np.arange(KL, dtype=np.float64), 0.0)
        ang0 = pos0[None, :] * invf0[:, None]
        c0t = np.cos(ang0)
        s0t = np.sin(ang0)
        cos0c = np.concatenate([c0t, c0t], axis=0).astype(f32)
        sin0c = np.concatenate([-s0t, s0t], axis=0).astype(f32)
        d = dict(shared)
        d["xT"] = xTc
        d["msw"] = msk.reshape(8 * 128, 640).astype(BFD)
        d["mom"] = momk.reshape(8 * 128, 256).astype(BFD)
        d["m1"] = m1c.reshape(8 * 128, S).astype(BFD)
        d["cos0"] = cos0c
        d["sin0"] = sin0c
        d["cos1q"] = np.ascontiguousarray(cos1k[:, qstart:qstart + QN]) * 0.125
        d["sin1q"] = np.ascontiguousarray(sin1k[:, qstart:qstart + QN]) * 0.125
        per_core.append(d)
    return per_core


_NC_CACHE = {}

MODE = "two"

L0_KEYS = ["identI", "ones1f", "onescf", "onescb", "n1r", "n2r", "qsc", "ksc",
           "xT", "msw", "mom", "cos0", "sin0", "dect", "w_qkv", "gg_w1",
           "gg_w2", "w_o", "mem_w1", "mem_w2", "ffn_w1", "ffn_w3", "ffn_w2",
           "wg_t", "wu_t", "wd_t"]
L1_KEYS = ["identI", "ones1f", "onescf", "onescb", "sc1r", "b1r", "m1",
           "cos1k", "sin1k", "cos1q", "sin1q", "wq_p2", "wk_p", "wv", "wo_p",
           "w1f_t"]


def _get_nc(part):
    if part not in _NC_CACHE:
        _NC_CACHE[part] = build_nc(part)
    return _NC_CACHE[part]


def kernel(**inputs):
    per_core = _stage(inputs)
    out = np.empty((4, S, D), np.float32)
    if MODE == "fused":
        nc = _get_nc("fused")
        res = run_bass_kernel_spmd(nc, per_core, core_ids=list(range(8)))
        for c in range(8):
            b, qh = c // 2, c % 2
            out[b, qh * 1024:(qh + 1) * 1024, :] = np.asarray(res.results[c]["outT"]).T
        return out
    nc0 = _get_nc("l0")
    in0 = [{k: d[k] for k in L0_KEYS} for d in per_core]
    r0 = run_bass_kernel_spmd(nc0, in0, core_ids=list(range(8)))
    nc1 = _get_nc("l1")
    in1 = []
    for c in range(8):
        b = c // 2
        x1all = np.ascontiguousarray(np.concatenate(
            [np.asarray(r0.results[2 * b]["x1out"]),
             np.asarray(r0.results[2 * b + 1]["x1out"])], axis=0))
        m = {k: per_core[c][k] for k in L1_KEYS}
        m["accin"] = np.ascontiguousarray(np.asarray(r0.results[c]["accout"]))
        m["x1in"] = x1all
        in1.append(m)
    r1 = run_bass_kernel_spmd(nc1, in1, core_ids=list(range(8)))
    for c in range(8):
        b, qh = c // 2, c % 2
        out[b, qh * 1024:(qh + 1) * 1024, :] = np.asarray(r1.results[c]["outT"]).T
    return out
